# version 04
# baseline (speedup 1.0000x reference)
import numpy as np
import concourse.bass as bass
import concourse.mybir as mybir
from concourse.bass_utils import run_bass_kernel_spmd
from contextlib import ExitStack

F32 = mybir.dt.float32
BF16 = mybir.dt.bfloat16
I32 = mybir.dt.int32
U8 = mybir.dt.uint8
AF = mybir.ActivationFunctionType
ALU = mybir.AluOpType
AX = mybir.AxisListType

NCORE = 8
T = 2048
D = 4096
IC = 32
G = 16
KF = 1024
NH = 32
NKV = 8
EPS_LN = 1e-5
EPS_NRM = 1e-10


def build_nc(upto=99):
    nc = bass.Bass("TRN2", target_bir_lowering=False, debug=False)

    t_x = nc.dram_tensor("x", [T, D], F32, kind="ExternalInput")
    t_wq = nc.dram_tensor("wq", [D, D], F32, kind="ExternalInput")
    t_bq = nc.dram_tensor("bq", [D], F32, kind="ExternalInput")
    t_wk = nc.dram_tensor("wk", [KF, D], F32, kind="ExternalInput")
    t_bk = nc.dram_tensor("bk", [KF], F32, kind="ExternalInput")
    t_wv = nc.dram_tensor("wv", [KF, D], F32, kind="ExternalInput")
    t_bv = nc.dram_tensor("bv", [KF], F32, kind="ExternalInput")
    t_wo = nc.dram_tensor("wo", [D, D], F32, kind="ExternalInput")
    t_bo = nc.dram_tensor("bo", [D], F32, kind="ExternalInput")
    t_wdq = nc.dram_tensor("wdq", [D, D], I32, kind="ExternalInput")
    t_sc = nc.dram_tensor("scales", [D, 128], F32, kind="ExternalInput")
    t_id = nc.dram_tensor("ident", [128, 128], F32, kind="ExternalInput")
    t_out = nc.dram_tensor("out", [T, D], F32, kind="ExternalOutput")

    wq_bf = nc.dram_tensor("wq_bf", [D, D], BF16)
    wk_bf = nc.dram_tensor("wk_bf", [KF, D], BF16)
    wv_bf = nc.dram_tensor("wv_bf", [KF, D], BF16)
    wo_bf = nc.dram_tensor("wo_bf", [D, D], BF16)
    wd_bf = nc.dram_tensor("wd_bf", [D, D], BF16)
    xn_dram = nc.dram_tensor("xn_dram", [T, D], BF16)
    kphi_dram = nc.dram_tensor("kphi_dram", [T, KF], BF16)
    vbf_dram = nc.dram_tensor("vbf_dram", [T, KF], BF16)
    g_dram = nc.dram_tensor("g_dram", [T, D], F32)
    attn_dram = nc.dram_tensor("attn_dram", [NH, 128, T], BF16)
    cc_in = nc.dram_tensor("cc_in", [128, 1032], F32)
    cc_out = nc.dram_tensor("cc_out", [128, 1032], F32)
    rbounce = nc.dram_tensor("rbounce", [8, 128], F32)

    ES = ExitStack()
    E = ES.enter_context

    slab = E(nc.sbuf_tensor("slab", [128, IC, T], BF16))
    wstrip = E(nc.sbuf_tensor("wstrip", [128, IC, 512], BF16))
    arena = E(nc.sbuf_tensor("arena", [128, 24576], U8))
    kvp = E(nc.sbuf_tensor("kvp", [128, NKV, 128], BF16))
    ident_sb = E(nc.sbuf_tensor("ident_sb", [128, 128], F32))
    ones_sb = E(nc.sbuf_tensor("ones_sb", [1, 128], F32))
    bq_cols = E(nc.sbuf_tensor("bq_cols", [128, 32], F32))
    bq_nat = E(nc.sbuf_tensor("bq_nat", [32, 128], F32))
    b_cur = E(nc.sbuf_tensor("b_cur", [1, 512], F32))
    ub = [E(nc.sbuf_tensor(f"ub{i}", [128, 512], F32)) for i in range(2)]
    eb = [E(nc.sbuf_tensor(f"eb{i}", [128, 512], F32)) for i in range(2)]
    ab_ = [E(nc.sbuf_tensor(f"abf{i}", [128, 512], F32)) for i in range(2)]
    fq = [E(nc.sbuf_tensor(f"fq{i}", [128, 512], BF16)) for i in range(2)]
    aq = [E(nc.sbuf_tensor(f"aq{i}", [128, 512], BF16)) for i in range(2)]
    stat = E(nc.sbuf_tensor("stat", [128, 8, G], F32))
    SSUM, SACC, SM, SMSQ, SVAR, SLNV, SRSTD, SNMR = range(8)
    kscol = E(nc.sbuf_tensor("kscol", [128, 8], F32))
    rrec = E(nc.sbuf_tensor("rrec", [8, 128], F32))

    pq = [E(nc.psum_tensor(f"pq{i}", [128, 512], F32)) for i in range(2)]
    pa = [E(nc.psum_tensor(f"pa{i}", [128, 512], F32)) for i in range(2)]

    S = {}
    N = {}
    COMPUTE = ("PE", "ACT", "DVE", "PL", "CC", "B")
    DMA_SEMS = (
        "XL", "TL", "WDL", "STR", "BC",
        "KT0", "KT1", "GPP0", "GPP1", "XPP0", "XPP1", "ARL",
        "XNS", "WDS", "KST0", "KST1", "GST0", "GST1", "AST0", "AST1", "CCS",
        "CWK", "CWV",
        "CQ0", "CQ1", "CQ2", "CQ3", "CQ4", "CQ5", "CQ6", "CQ7",
        "CO0", "CO1", "CO2", "CO3", "CO4", "CO5", "CO6", "CO7",
        "OUT0", "OUT1",
    )
    for name in COMPUTE + DMA_SEMS:
        S[name] = E(nc.semaphore(name))
        N[name] = 0

    block = E(nc.Block())

    def carve(off_bytes, shape, dt):
        nbytes = int(np.prod(shape[1:])) * mybir.dt.size(dt)
        assert off_bytes % 64 == 0
        assert off_bytes + nbytes <= 24576, (off_bytes, nbytes)
        v = arena.ap()[:, off_bytes:off_bytes + nbytes].bitcast(dt)
        if len(shape) == 2:
            return v
        pat = "p (" + " ".join(f"d{i}" for i in range(1, len(shape))) + ") -> p " + \
              " ".join(f"d{i}" for i in range(1, len(shape)))
        kw = {f"d{i}": shape[i] for i in range(1, len(shape))}
        return v.rearrange(pat, **kw)

    ENGS = ("sync", "gpsimd", "tensor", "scalar", "vector")
    TRACE = {k: [] for k in ENGS}
    nc._plan_trace = TRACE

    class Phase:
        def __init__(self):
            self.ops = {k: [] for k in ENGS}

        def add(self, eng, waits, build, incs=()):
            for sname, k in incs:
                N[sname] += k
            after = {sname: N[sname] for sname, k in incs}
            wl = [(S[sname], v) for sname, v in waits if v > 0]
            il = [(S[sname], k) for sname, k in incs]
            TRACE[eng].append(([(sn, v) for sn, v in waits if v > 0],
                               [(sn, k) for sn, k in incs]))

            def f(e, wl=wl, build=build, il=il):
                for sem, v in wl:
                    e.wait_ge(sem, v)
                ins = build(e)
                for sem, k in il:
                    ins = ins.then_inc(sem, k)
            self.ops[eng].append(f)
            return after

        def dma(self, eng, stream, waits, build):
            self.add(eng, waits, build, [(stream, 16)])
            return (stream, N[stream])

        def flush(self):
            for eng in ENGS:
                lst = self.ops[eng]
                if not lst:
                    continue
                getattr(block, eng)(lambda e, lst=lst: [f(e) for f in lst])
            self.ops = {k: [] for k in ENGS}

    SYNC_STREAMS = ("XL", "TL", "WDL", "STR", "BC", "KT0", "KT1",
                    "GPP0", "GPP1", "XPP0", "XPP1", "ARL")
    GP_STREAMS = tuple(s for s in DMA_SEMS if s not in SYNC_STREAMS)

    def barrier(ph):
        ph.add("sync", [(s, N[s]) for s in SYNC_STREAMS], lambda e: e.nop(), [("B", 1)])
        ph.add("gpsimd", [(s, N[s]) for s in GP_STREAMS], lambda e: e.nop(), [("B", 1)])
        for eng in ("tensor", "scalar", "vector"):
            ph.add(eng, [], lambda e: e.nop(), [("B", 1)])
        tgt = N["B"]
        for eng in ENGS:
            ph.add(eng, [("B", tgt)], lambda e: e.nop())

    def load_full_set(e):
        return e.add_instruction(mybir.InstLoadActFuncSet(
            name=nc.get_next_instruction_name(), act_func_set_id=6, ins=[], outs=[]))

    R = {}

    ph = Phase()
    ph.dma("sync", "BC", [], lambda e: e.dma_start(out=ident_sb[:, :], in_=t_id.ap()))
    tok_bq = ph.dma("sync", "BC", [], lambda e: e.dma_start(
        out=bq_nat[:, :], in_=t_bq.ap().rearrange("(c p) -> c p", p=128)))
    R["cast_wk"] = ph.dma("gpsimd", "CWK", [], lambda e: e.dma_start(out=wk_bf.ap(), in_=t_wk.ap()))
    R["cast_wv"] = ph.dma("gpsimd", "CWV", [], lambda e: e.dma_start(out=wv_bf.ap(), in_=t_wv.ap()))
    ph.add("vector", [], lambda e: e.memset(ones_sb[:, :], 1.0), [("DVE", 1)])
    pe1 = ph.add("tensor", [tok_bq],
                 lambda e: e.matmul(pa[0][:, 0:32], bq_nat[:, :], ident_sb[0:32, 0:32],
                                    start=True, stop=True), [("PE", 1)])["PE"]
    ph.add("vector", [("PE", pe1)],
           lambda e: e.tensor_copy(bq_cols[:, :], pa[0][:, 0:32]), [("DVE", 1)])
    ph.add("scalar", [], lambda e: load_full_set(e))
    barrier(ph)
    ph.flush()

    if upto < 1:
        ES.close()
        return nc

    ph = Phase()
    ph.add("scalar", [], lambda e: load_full_set(e))
    xbuf = carve(0, [128, D], F32)
    xn_tm = carve(16384, [128, D], BF16)
    inv_d = 1.0 / D
    ident_c = {}
    store_tok = {}
    for g in range(G):
        xl = ph.dma("sync", "XL", [("ACT", ident_c.get(g - 1, 0))],
                    lambda e, g=g: e.dma_start(out=xbuf, in_=t_x.ap()[g * 128:(g + 1) * 128, :]))
        red = ph.add("vector", [xl], lambda e, g=g: e.reduce_sum(
            stat[:, SSUM, g:g + 1], xbuf, axis=AX.X), [("DVE", 1)])["DVE"]
        w = [xl] + ([store_tok[g - 1]] if g >= 1 else [])
        sq = ph.add("scalar", w, lambda e, g=g: e.activation(
            xn_tm, xbuf, AF.Square, accum_out=stat[:, SACC, g:g + 1]), [("ACT", 1)])["ACT"]
        m = ph.add("vector", [("DVE", red)], lambda e, g=g: e.tensor_scalar_mul(
            stat[:, SM, g:g + 1], stat[:, SSUM, g:g + 1], inv_d), [("DVE", 1)])["DVE"]
        msq = ph.add("vector", [("DVE", m), ("ACT", sq)], lambda e, g=g: e.tensor_scalar(
            stat[:, SMSQ, g:g + 1], stat[:, SM, g:g + 1], stat[:, SM, g:g + 1],
            float(-EPS_LN), ALU.mult, ALU.add), [("DVE", 1)])["DVE"]
        var = ph.add("vector", [("DVE", msq)], lambda e, g=g: e.scalar_tensor_tensor(
            stat[:, SVAR, g:g + 1], stat[:, SACC, g:g + 1], inv_d, stat[:, SMSQ, g:g + 1],
            ALU.mult, ALU.subtract), [("DVE", 1)])["DVE"]
        ln = ph.add("scalar", [("DVE", var)], lambda e, g=g: e.activation(
            stat[:, SLNV, g:g + 1], stat[:, SVAR, g:g + 1], AF.Ln), [("ACT", 1)])["ACT"]
        nh = ph.add("vector", [("ACT", ln)], lambda e, g=g: e.tensor_scalar_mul(
            stat[:, SLNV, g:g + 1], stat[:, SLNV, g:g + 1], -0.5), [("DVE", 1)])["DVE"]
        rstd = ph.add("scalar", [("DVE", nh)], lambda e, g=g: e.activation(
            stat[:, SRSTD, g:g + 1], stat[:, SLNV, g:g + 1], AF.Exp), [("ACT", 1)])["ACT"]
        nmr = ph.add("vector", [("ACT", rstd)], lambda e, g=g: e.scalar_tensor_tensor(
            stat[:, SNMR, g:g + 1], stat[:, SM, g:g + 1], -1.0, stat[:, SRSTD, g:g + 1],
            ALU.mult, ALU.mult), [("DVE", 1)])["DVE"]
        ident_c[g] = ph.add("scalar", [("DVE", nmr)], lambda e, g=g: e.activation(
            xn_tm, xbuf, AF.Identity,
            bias=stat[:, SNMR, g:g + 1], scale=stat[:, SRSTD, g:g + 1]), [("ACT", 1)])["ACT"]
        store_tok[g] = ph.dma("gpsimd", "XNS", [("ACT", ident_c[g])],
                              lambda e, g=g: e.dma_start(
                                  out=xn_dram.ap()[g * 128:(g + 1) * 128, :], in_=xn_tm))
        if g % 4 == 3:
            tb = g // 4
            for ic in range(IC):
                tok = ph.dma("sync", "TL", [store_tok[g]] if ic == 0 else [],
                             lambda e, tb=tb, ic=ic: e.dma_start(
                                 out=slab[:, ic, tb * 512:(tb + 1) * 512],
                                 in_=xn_dram.ap()[tb * 512:(tb + 1) * 512, ic * 128:(ic + 1) * 128],
                                 transpose=True))
            R["slab_ready"] = tok
    barrier(ph)
    ph.flush()

    if upto < 2:
        ES.close()
        return nc

    ph = Phase()
    wd_int = carve(0, [128, 32, 32], I32)
    wd_out = carve(4096, [128, 32, 32], BF16)
    wd_sc = carve(6208, [128, 32], F32)
    dq_c = {}
    wds_tok = {}
    for i in range(128):
        os_, cc = divmod(i, 4)
        ph.dma("sync", "WDL", [("DVE", dq_c.get(i - 1, 0))],
               lambda e, os_=os_, cc=cc: e.dma_start(
                   out=wd_int, in_=t_wdq.ap()[os_ * 128:(os_ + 1) * 128, cc * 1024:(cc + 1) * 1024]
                   .rearrange("p (b k) -> p b k", k=32)))
        dl = ph.dma("sync", "WDL", [], lambda e, os_=os_, cc=cc: e.dma_start(
            out=wd_sc, in_=t_sc.ap()[os_ * 128:(os_ + 1) * 128, cc * 32:(cc + 1) * 32]))
        dq_c[i] = ph.add("vector", [dl] + ([wds_tok[i - 1]] if i >= 1 else []),
                         lambda e: e.scalar_tensor_tensor(
                             wd_out, wd_int, -128.0,
                             wd_sc.unsqueeze(2).broadcast_to([128, 32, 32]),
                             ALU.add, ALU.mult), [("DVE", 1)])["DVE"]
        wds_tok[i] = ph.dma("gpsimd", "WDS", [("DVE", dq_c[i])],
                            lambda e, os_=os_, cc=cc: e.dma_start(
                                out=wd_bf.ap()[os_ * 128:(os_ + 1) * 128, cc * 1024:(cc + 1) * 1024],
                                in_=wd_out.rearrange("p b k -> p (b k)")))
        if i % 16 == 0:
            blk = i // 16
            R[f"cast_wq{blk}"] = ph.dma("gpsimd", f"CQ{blk}", [],
                                        lambda e, blk=blk: e.dma_start(
                                            out=wq_bf.ap()[blk * 512:(blk + 1) * 512, :],
                                            in_=t_wq.ap()[blk * 512:(blk + 1) * 512, :]))
    barrier(ph)
    ph.flush()

    if upto < 3:
        ES.close()
        return nc

    ph = Phase()
    ph.add("scalar", [], lambda e: load_full_set(e))
    chunks = [(True, 0), (True, 1), (False, 0), (False, 1)]
    pq_free = {}
    phi_c = {}
    ract_c = {}
    kst_tok = {}
    last_mm = 0
    for ci, (is_k, half) in enumerate(chunks):
        wsrc = wk_bf if is_k else wv_bf
        bsrc = t_bk if is_k else t_bv
        bl = ph.dma("sync", "BC", [("PE", last_mm)],
                    lambda e, bsrc=bsrc, half=half: e.dma_start(
                        out=b_cur[:, :], in_=bsrc.ap()[half * 512:(half + 1) * 512].unsqueeze(0)))
        stl = None
        for ic in range(IC):
            w = []
            if ic == 0:
                w = [("PE", last_mm), R["cast_wk"] if is_k else R["cast_wv"]]
            stl = ph.dma("sync", "STR", w, lambda e, wsrc=wsrc, half=half, ic=ic: e.dma_start(
                out=wstrip[:, ic, :],
                in_=wsrc.ap()[half * 512:(half + 1) * 512, ic * 128:(ic + 1) * 128],
                transpose=True))
        for g in range(G):
            ti = ci * G + g
            p = ti % 2
            q = ti % 2
            w = [stl, bl]
            if (ti - 2) in pq_free:
                w.append(pq_free[ti - 2])
            if ti == 0:
                w.append(R["slab_ready"])
            for ic in range(IC):
                ph.add("tensor", w if ic == 0 else [], lambda e, p=p, ic=ic, g=g: e.matmul(
                    pq[p][:, :], slab[:, ic, g * 128:(g + 1) * 128], wstrip[:, ic, :],
                    start=(ic == 0), stop=False))
            mm = ph.add("tensor", [], lambda e, p=p: e.matmul(
                pq[p][:, :], ones_sb[:, :], b_cur[:, :], start=False, stop=True),
                [("PE", 1)])["PE"]
            last_mm = mm
            if is_k:
                ucp = ph.add("vector", [("PE", mm), ("ACT", ract_c.get(ti - 2, 0))],
                             lambda e, p=p, q=q: e.tensor_copy(ub[q][:, :], pq[p][:, :]),
                             [("DVE", 1)])["DVE"]
                pq_free[ti] = ("DVE", ucp)
                ph.add("scalar", [("DVE", ucp), ("DVE", phi_c.get(ti - 2, 0))],
                       lambda e, q=q: e.activation(eb[q][:, :], ub[q][:, :], AF.Exp))
                ract_c[ti] = ph.add("scalar", [], lambda e, q=q: e.activation(
                    ab_[q][:, :], ub[q][:, :], AF.Relu), [("ACT", 1)])["ACT"]
                wph = [("ACT", ract_c[ti])]
                if (ti - 2) in kst_tok:
                    wph.append(kst_tok[ti - 2])
                phi_c[ti] = ph.add("vector", wph,
                                   lambda e, q=q: e.scalar_tensor_tensor(
                                       fq[q][:, :], eb[q][:, :], 1.0, ab_[q][:, :],
                                       ALU.min, ALU.add), [("DVE", 1)])["DVE"]
                done = phi_c[ti]
            else:
                wv_ = [("PE", mm)]
                if (ti - 2) in kst_tok:
                    wv_.append(kst_tok[ti - 2])
                done = ph.add("vector", wv_,
                              lambda e, p=p, q=q: e.tensor_copy(
                                  fq[q][:, :], pq[p][:, :]), [("DVE", 1)])["DVE"]
                pq_free[ti] = ("DVE", done)
            dst = kphi_dram if is_k else vbf_dram
            kst_tok[ti] = ph.dma("gpsimd", f"KST{ti % 2}", [("DVE", done)],
                                 lambda e, g=g, half=half, q=q, dst=dst: e.dma_start(
                                     out=dst.ap()[g * 128:(g + 1) * 128, half * 512:(half + 1) * 512],
                                     in_=fq[q][:, :]))
    barrier(ph)
    ph.flush()

    if upto < 4:
        ES.close()
        return nc

    ph = Phase()
    kt = [carve(0, [128, KF], BF16), carve(2048, [128, KF], BF16)]
    vaug = [carve(4096, [128, 8, 144], BF16), carve(6400, [128, 8, 144], BF16)]
    kv_acc = carve(8704, [128, 1032], F32)
    with nc.psum_tensor("pkv", [128, 1024], F32) as pkv, \
         nc.psum_tensor("pks", [128, 8], F32) as pks:
        ph.add("vector", [], lambda e: e.memset(kv_acc, 0.0), [("DVE", 1)])
        ph.add("vector", [], lambda e: e.memset(vaug[0][:, :, 128:144], 1.0), [("DVE", 1)])
        mset = ph.add("vector", [], lambda e: e.memset(vaug[1][:, :, 128:144], 1.0), [("DVE", 1)])["DVE"]
        kvg = {}
        accg = {}
        for g in range(G):
            b = g % 2
            ph.dma("sync", f"KT{b}", [("PE", kvg.get(g - 2, 0))],
                   lambda e, g=g, b=b: e.dma_start(
                       out=kt[b], in_=kphi_dram.ap()[g * 128:(g + 1) * 128, :]))
            ktl = ph.dma("sync", f"KT{b}", [], lambda e, g=g, b=b: e.dma_start(
                out=vaug[b][:, :, 0:128],
                in_=vbf_dram.ap()[g * 128:(g + 1) * 128, :].rearrange("l (h v) -> l h v", h=8)))
            w = [ktl, ("DVE", accg.get(g - 1, mset))]
            for h in range(NKV):
                ph.add("tensor", w if h == 0 else [],
                       lambda e, b=b, h=h: e.matmul(
                           pkv.ap()[:, h * 128:(h + 1) * 128],
                           kt[b][:, h * 128:(h + 1) * 128],
                           vaug[b][:, h, 0:128], start=True, stop=True))
                kvg[g] = ph.add("tensor", [],
                                lambda e, b=b, h=h: e.matmul(
                                    pks.ap()[:, h:h + 1],
                                    kt[b][:, h * 128:(h + 1) * 128],
                                    vaug[b][:, h, 128:129], start=True, stop=True),
                                [("PE", 1)])["PE"]
            ph.add("vector", [("PE", kvg[g])], lambda e: e.tensor_tensor(
                kv_acc[:, 0:1024], kv_acc[:, 0:1024], pkv.ap()[:, :], ALU.add), [("DVE", 1)])
            accg[g] = ph.add("vector", [], lambda e: e.tensor_tensor(
                kv_acc[:, 1024:1032], kv_acc[:, 1024:1032], pks.ap()[:, :], ALU.add),
                [("DVE", 1)])["DVE"]
        ccst = ph.dma("gpsimd", "CCS", [("DVE", accg[G - 1])],
                      lambda e: e.dma_start(out=cc_in.ap(), in_=kv_acc))
        ph.add("gpsimd", [ccst], lambda e: e.collective_compute(
            "AllReduce", ALU.add, replica_groups=[list(range(NCORE))],
            ins=[cc_in.ap()], outs=[cc_out.ap()]), [("CC", 1)])
        ph.flush()

    if upto < 5:
        ES.close()
        return nc

    ph = Phase()
    ph.add("scalar", [], lambda e: e.preload_act_table(AF.Silu))
    silu_c = {}
    gst_tok = {}
    last_mm_f = 0
    idx = 0
    for ow in range(8):
        stl = None
        for ic in range(IC):
            w = [("PE", last_mm_f)] if ic == 0 else []
            stl = ph.dma("sync", "STR", w, lambda e, ow=ow, ic=ic: e.dma_start(
                out=wstrip[:, ic, :],
                in_=wd_bf.ap()[ow * 512:(ow + 1) * 512, ic * 128:(ic + 1) * 128],
                transpose=True))
        for g in range(G):
            p = idx % 2
            w = [stl, ("ACT", silu_c.get(idx - 2, 0))]
            for ic in range(IC):
                inc = [("PE", 1)] if ic == IC - 1 else ()
                ph.add("tensor", w if ic == 0 else [], lambda e, p=p, ic=ic, g=g: e.matmul(
                    pq[p][:, :], slab[:, ic, g * 128:(g + 1) * 128], wstrip[:, ic, :],
                    start=(ic == 0), stop=(ic == IC - 1)), inc)
            last_mm_f = N["PE"]
            wsl = [("PE", last_mm_f)]
            if (idx - 2) in gst_tok:
                wsl.append(gst_tok[idx - 2])
            silu_c[idx] = ph.add("scalar", wsl, lambda e, p=p: e.activation(
                eb[p][:, :], pq[p][:, :], AF.Silu), [("ACT", 1)])["ACT"]
            gst_tok[idx] = ph.dma("gpsimd", f"GST{idx % 2}", [("ACT", silu_c[idx])],
                                  lambda e, ow=ow, g=g, p=p: e.dma_start(
                                      out=g_dram.ap()[g * 128:(g + 1) * 128, ow * 512:(ow + 1) * 512],
                                      in_=eb[p][:, :]))
            idx += 1
        R[f"cast_wo{ow}"] = ph.dma("gpsimd", f"CO{ow}", [], lambda e, ow=ow: e.dma_start(
            out=wo_bf.ap()[ow * 512:(ow + 1) * 512, :],
            in_=t_wo.ap()[ow * 512:(ow + 1) * 512, :]))
    R["ffn_last_silu"] = silu_c[idx - 1]
    barrier(ph)
    ph.flush()

    if upto < 6:
        ES.close()
        return nc

    ph = Phase()
    kv_all = carve(13312, [128, 1032], F32)
    kvl = ph.dma("sync", "BC", [("CC", 1)], lambda e: e.dma_start(
        out=kv_all, in_=cc_out.ap()))
    c1 = ph.add("vector", [kvl], lambda e: e.tensor_copy(
        kscol[:, :], kv_all[:, 1024:1032]), [("DVE", 1)])["DVE"]
    p1 = ph.add("tensor", [("DVE", c1)], lambda e: e.matmul(
        pa[0][0:8, 0:128], kscol[:, :], ident_sb[:, :], start=True, stop=True), [("PE", 1)])["PE"]
    c2 = ph.add("vector", [("PE", p1)], lambda e: e.tensor_scalar_add(
        rrec[:, :], pa[0][0:8, 0:128], float(EPS_NRM)), [("DVE", 1)])["DVE"]
    c3 = ph.add("vector", [("DVE", c2)], lambda e: e.reciprocal(rrec[:, :], rrec[:, :]),
                [("DVE", 1)])["DVE"]
    g1 = ph.dma("gpsimd", "CCS", [("DVE", c3)],
                lambda e: e.dma_start(out=rbounce.ap(), in_=rrec[:, :]))
    rflat = rbounce.ap().rearrange("h v -> (h v)")
    l2 = ph.dma("sync", "BC", [g1], lambda e: e.dma_start(
        out=b_cur[:, :], in_=rflat[0:512].unsqueeze(0)))
    p2 = ph.add("tensor", [l2, ("ACT", R["ffn_last_silu"])], lambda e: e.matmul(
        pq[0][:, :], ones_sb[:, :], b_cur[:, :], start=True, stop=True), [("PE", 1)])["PE"]
    l3 = ph.dma("sync", "BC", [("PE", p2)], lambda e: e.dma_start(
        out=b_cur[:, :], in_=rflat[512:1024].unsqueeze(0)))
    p3 = ph.add("tensor", [l3], lambda e: e.matmul(
        pq[1][:, :], ones_sb[:, :], b_cur[:, :], start=True, stop=True), [("PE", 1)])["PE"]
    ph.add("vector", [("PE", p3)], lambda e: e.tensor_tensor(
        kvp[:, 0:4, :], kv_all[:, 0:512].rearrange("p (h v) -> p h v", h=4),
        pq[0].ap().rearrange("p (h v) -> p h v", h=4), ALU.mult), [("DVE", 1)])
    kvp_done = ph.add("vector", [], lambda e: e.tensor_tensor(
        kvp[:, 4:8, :], kv_all[:, 512:1024].rearrange("p (h v) -> p h v", h=4),
        pq[1].ap().rearrange("p (h v) -> p h v", h=4), ALU.mult), [("DVE", 1)])["DVE"]
    R["kvp_done"] = kvp_done
    barrier(ph)
    ph.flush()

    if upto < 7:
        ES.close()
        return nc

    ph = Phase()
    ph.add("scalar", [], lambda e: load_full_set(e))
    qrelu = {}
    pqa_free = {}
    phiq = {}
    amm = {}
    abcp = {}
    ast_tok = {}
    last_qmm = 0
    pending = None
    idx = 0

    def emit_attn(pidx, h_, pp, qq):
        wa = [("DVE", phiq[pidx]), ("DVE", abcp.get(pidx - 2, 0))]
        if pidx < 2:
            wa.append(("DVE", R["kvp_done"]))
        amm[pidx] = ph.add("tensor", wa,
                           lambda e, h_=h_, pp=pp, qq=qq: e.matmul(
                               pa[pp][:, :], kvp[:, h_ // 4, :], fq[qq][:, :],
                               start=True, stop=True), [("PE", 1)])["PE"]
        wc = [("PE", amm[pidx])]
        if (pidx - 2) in ast_tok:
            wc.append(ast_tok[pidx - 2])
        abcp[pidx] = ph.add("vector", wc,
                            lambda e, pp=pp, qq=qq: e.tensor_copy(aq[qq][:, :], pa[pp][:, :]),
                            [("DVE", 1)])["DVE"]
        pht, ptc = divmod(pidx, 4)
        ast_tok[pidx] = ph.dma("gpsimd", f"AST{pidx % 2}", [("DVE", abcp[pidx])],
                               lambda e, pht=pht, ptc=ptc, qq=qq: e.dma_start(
                                   out=attn_dram.ap()[pht, :, ptc * 512:(ptc + 1) * 512],
                                   in_=aq[qq][:, :]))

    for hg in range(8):
        stl = None
        for ic in range(IC):
            w = []
            if ic == 0:
                w = [("PE", last_qmm), R[f"cast_wq{hg}"]]
            stl = ph.dma("sync", "STR", w, lambda e, hg=hg, ic=ic: e.dma_start(
                out=wstrip[:, ic, :],
                in_=wq_bf.ap()[hg * 512:(hg + 1) * 512, ic * 128:(ic + 1) * 128],
                transpose=True))
        for h4 in range(4):
            h = hg * 4 + h4
            for tc in range(4):
                p = idx % 2
                q = idx % 2
                w = [stl]
                if (idx - 2) in pqa_free:
                    w.append(pqa_free[idx - 2])
                if idx < 2:
                    w.append(("DVE", R["kvp_done"]))
                for ic in range(IC):
                    inc = [("PE", 1)] if ic == IC - 1 else ()
                    ph.add("tensor", w if ic == 0 else [], lambda e, p=p, ic=ic, h4=h4, tc=tc: e.matmul(
                        pq[p][:, :], wstrip[:, ic, h4 * 128:(h4 + 1) * 128],
                        slab[:, ic, tc * 512:(tc + 1) * 512],
                        start=(ic == 0), stop=(ic == IC - 1)), inc)
                qmm = N["PE"]
                last_qmm = qmm
                if pending is not None:
                    emit_attn(*pending)
                qcp = ph.add("vector", [("PE", qmm), ("ACT", qrelu.get(idx - 2, 0))],
                             lambda e, p=p, q=q, h=h: e.tensor_scalar_add(
                                 ub[q][:, :], pq[p][:, :], bq_cols[:, h:h + 1]),
                             [("DVE", 1)])["DVE"]
                pqa_free[idx] = ("DVE", qcp)
                ph.add("scalar", [("DVE", qcp), ("DVE", phiq.get(idx - 2, 0))],
                       lambda e, q=q: e.activation(eb[q][:, :], ub[q][:, :], AF.Exp))
                qrelu[idx] = ph.add("scalar", [], lambda e, q=q: e.activation(
                    ab_[q][:, :], ub[q][:, :], AF.Relu), [("ACT", 1)])["ACT"]
                phiq[idx] = ph.add("vector", [("ACT", qrelu[idx]), ("PE", amm.get(idx - 2, 0))],
                                   lambda e, q=q: e.scalar_tensor_tensor(
                                       fq[q][:, :], eb[q][:, :], 1.0, ab_[q][:, :],
                                       ALU.min, ALU.add), [("DVE", 1)])["DVE"]
                pending = (idx, h, p, q)
                idx += 1
    emit_attn(*pending)
    barrier(ph)
    ph.flush()

    if upto < 8:
        ES.close()
        return nc

    ph = Phase()
    gp_piece = [carve(0, [128, 512], F32), carve(2048, [128, 512], F32)]
    xp_piece = [carve(4096, [128, 512], F32), carve(6144, [128, 512], F32)]
    outb = [carve(8192, [128, 512], F32), carve(10240, [128, 512], F32)]
    arl = None
    for tc in range(4):
        arl = ph.dma("sync", "ARL", [], lambda e, tc=tc: e.dma_start(
            out=slab[:, :, tc * 512:(tc + 1) * 512],
            in_=attn_dram.ap()[:, :, tc * 512:(tc + 1) * 512].rearrange("fc p t -> p fc t")))
    prod = {}
    outtt = {}
    out_tok = {}
    last_smm = 0
    idx = 0
    for ow in range(8):
        bl = ph.dma("sync", "BC", [("PE", last_smm)], lambda e, ow=ow: e.dma_start(
            out=b_cur[:, :], in_=t_bo.ap()[ow * 512:(ow + 1) * 512].unsqueeze(0)))
        stl = None
        for fc in range(IC):
            w = []
            if fc == 0:
                w = [("PE", last_smm), R[f"cast_wo{ow}"]]
            stl = ph.dma("sync", "STR", w, lambda e, ow=ow, fc=fc: e.dma_start(
                out=wstrip[:, fc, :],
                in_=wo_bf.ap()[ow * 512:(ow + 1) * 512, fc * 128:(fc + 1) * 128],
                transpose=True))
        for g in range(G):
            p = idx % 2
            q = idx % 2
            gl = ph.dma("sync", f"GPP{q}", [("DVE", prod.get(idx - 2, 0))],
                        lambda e, ow=ow, g=g, q=q: e.dma_start(
                            out=gp_piece[q],
                            in_=g_dram.ap()[g * 128:(g + 1) * 128, ow * 512:(ow + 1) * 512]))
            xl = ph.dma("sync", f"XPP{q}", [("PL", outtt.get(idx - 2, 0))],
                        lambda e, ow=ow, g=g, q=q: e.dma_start(
                            out=xp_piece[q],
                            in_=t_x.ap()[g * 128:(g + 1) * 128, ow * 512:(ow + 1) * 512]))
            w = [stl, bl, ("DVE", prod.get(idx - 2, 0))]
            if idx == 0:
                w.append(arl)
            for fc in range(IC):
                ph.add("tensor", w if fc == 0 else [], lambda e, p=p, fc=fc, g=g: e.matmul(
                    pq[p][:, :], slab[:, fc, g * 128:(g + 1) * 128], wstrip[:, fc, :],
                    start=(fc == 0), stop=False))
            smm = ph.add("tensor", [], lambda e, p=p: e.matmul(
                pq[p][:, :], ones_sb[:, :], b_cur[:, :], start=False, stop=True),
                [("PE", 1)])["PE"]
            last_smm = smm
            prod[idx] = ph.add("vector", [("PE", smm), gl, ("PL", outtt.get(idx - 2, 0))],
                               lambda e, p=p, q=q: e.tensor_tensor(
                                   ub[q][:, :], pq[p][:, :], gp_piece[q], ALU.mult),
                               [("DVE", 1)])["DVE"]
            wo_ = [("DVE", prod[idx]), xl]
            if (idx - 2) in out_tok:
                wo_.append(out_tok[idx - 2])
            outtt[idx] = ph.add("gpsimd", wo_,
                                lambda e, q=q: e.tensor_tensor(
                                    outb[q], ub[q][:, :], xp_piece[q], ALU.add), [("PL", 1)])["PL"]
            out_tok[idx] = ph.dma("gpsimd", f"OUT{idx % 2}", [("PL", outtt[idx])],
                                  lambda e, ow=ow, g=g, q=q: e.dma_start(
                                      out=t_out.ap()[g * 128:(g + 1) * 128, ow * 512:(ow + 1) * 512],
                                      in_=outb[q]))
            idx += 1
    ph.add("gpsimd", [("OUT0", N["OUT0"]), ("OUT1", N["OUT1"])], lambda e: e.nop())
    ph.flush()

    ES.close()
    return nc


_NC_CACHE = None


def kernel(L0_attn_hidden_states, L0_attn_Wq, L0_attn_bq, L0_attn_Wk, L0_attn_bk,
           L0_attn_Wv, L0_attn_bv, L0_attn_Wo, L0_attn_bo, L0_ffn_w_q, L0_ffn_scales):
    global _NC_CACHE
    x = np.ascontiguousarray(np.asarray(L0_attn_hidden_states, dtype=np.float32).reshape(16384, 4096))
    wq = np.ascontiguousarray(np.asarray(L0_attn_Wq, dtype=np.float32))
    bq = np.ascontiguousarray(np.asarray(L0_attn_bq, dtype=np.float32))
    wk = np.ascontiguousarray(np.asarray(L0_attn_Wk, dtype=np.float32))
    bk = np.ascontiguousarray(np.asarray(L0_attn_bk, dtype=np.float32))
    wv = np.ascontiguousarray(np.asarray(L0_attn_Wv, dtype=np.float32))
    bv = np.ascontiguousarray(np.asarray(L0_attn_bv, dtype=np.float32))
    wo = np.ascontiguousarray(np.asarray(L0_attn_Wo, dtype=np.float32))
    bo = np.ascontiguousarray(np.asarray(L0_attn_bo, dtype=np.float32))
    wdq = np.ascontiguousarray(np.asarray(L0_ffn_w_q, dtype=np.int32))
    sc = np.ascontiguousarray(np.asarray(L0_ffn_scales, dtype=np.float32))
    ident = np.eye(128, dtype=np.float32)

    if _NC_CACHE is None:
        _NC_CACHE = build_nc()
    nc = _NC_CACHE

    in_maps = []
    for c in range(NCORE):
        in_maps.append({
            "x": x[c * T:(c + 1) * T],
            "wq": wq, "bq": bq, "wk": wk, "bk": bk, "wv": wv, "bv": bv,
            "wo": wo, "bo": bo, "wdq": wdq, "scales": sc, "ident": ident,
        })
    res = run_bass_kernel_spmd(nc, in_maps, core_ids=list(range(NCORE)))
    out = np.concatenate([np.asarray(res.results[c]["out"]) for c in range(NCORE)], axis=0)
    return out.reshape(1, 16384, 4096)


# version 05 (final)
# speedup vs baseline: 1.0217x; 1.0217x over previous
import numpy as np
import concourse.bass as bass
import concourse.mybir as mybir
from concourse.bass_utils import run_bass_kernel_spmd
from contextlib import ExitStack

F32 = mybir.dt.float32
BF16 = mybir.dt.bfloat16
I32 = mybir.dt.int32
U8 = mybir.dt.uint8
AF = mybir.ActivationFunctionType
ALU = mybir.AluOpType
AX = mybir.AxisListType

NCORE = 8
T = 2048
D = 4096
IC = 32
G = 16
KF = 1024
NH = 32
NKV = 8
EPS_LN = 1e-5
EPS_NRM = 1e-10


def build_nc(upto=99):
    nc = bass.Bass("TRN2", target_bir_lowering=False, debug=False)

    t_x = nc.dram_tensor("x", [T, D], F32, kind="ExternalInput")
    t_wq = nc.dram_tensor("wq", [D, D], F32, kind="ExternalInput")
    t_bq = nc.dram_tensor("bq", [D], F32, kind="ExternalInput")
    t_wk = nc.dram_tensor("wk", [KF, D], F32, kind="ExternalInput")
    t_bk = nc.dram_tensor("bk", [KF], F32, kind="ExternalInput")
    t_wv = nc.dram_tensor("wv", [KF, D], F32, kind="ExternalInput")
    t_bv = nc.dram_tensor("bv", [KF], F32, kind="ExternalInput")
    t_wo = nc.dram_tensor("wo", [D, D], F32, kind="ExternalInput")
    t_bo = nc.dram_tensor("bo", [D], F32, kind="ExternalInput")
    t_wdq = nc.dram_tensor("wdq", [D, D], I32, kind="ExternalInput")
    t_sc = nc.dram_tensor("scales", [D, 128], F32, kind="ExternalInput")
    t_id = nc.dram_tensor("ident", [128, 128], F32, kind="ExternalInput")
    t_out = nc.dram_tensor("out", [T, D], F32, kind="ExternalOutput")

    wq_bf = nc.dram_tensor("wq_bf", [D, D], BF16)
    wk_bf = nc.dram_tensor("wk_bf", [KF, D], BF16)
    wv_bf = nc.dram_tensor("wv_bf", [KF, D], BF16)
    wo_bf = nc.dram_tensor("wo_bf", [D, D], BF16)
    wd_bf = nc.dram_tensor("wd_bf", [D, D], BF16)
    xn_dram = nc.dram_tensor("xn_dram", [T, D], BF16)
    kphi_dram = nc.dram_tensor("kphi_dram", [T, KF], BF16)
    vbf_dram = nc.dram_tensor("vbf_dram", [T, KF], BF16)
    g_dram = nc.dram_tensor("g_dram", [T, D], F32)
    attn_dram = nc.dram_tensor("attn_dram", [NH, 128, T], BF16)
    cc_in = nc.dram_tensor("cc_in", [128, 1032], F32)
    cc_out = nc.dram_tensor("cc_out", [128, 1032], F32)
    rbounce = nc.dram_tensor("rbounce", [8, 128], F32)

    ES = ExitStack()
    E = ES.enter_context

    slab = E(nc.sbuf_tensor("slab", [128, IC, T], BF16))
    wstrip = E(nc.sbuf_tensor("wstrip", [128, IC, 512], BF16))
    arena = E(nc.sbuf_tensor("arena", [128, 24576], U8))
    kvp = E(nc.sbuf_tensor("kvp", [128, NKV, 128], BF16))
    ident_sb = E(nc.sbuf_tensor("ident_sb", [128, 128], F32))
    ones_sb = E(nc.sbuf_tensor("ones_sb", [1, 128], F32))
    bq_cols = E(nc.sbuf_tensor("bq_cols", [128, 32], F32))
    bq_nat = E(nc.sbuf_tensor("bq_nat", [32, 128], F32))
    b_cur = E(nc.sbuf_tensor("b_cur", [1, 512], F32))
    ub = [E(nc.sbuf_tensor(f"ub{i}", [128, 512], F32)) for i in range(2)]
    eb = [E(nc.sbuf_tensor(f"eb{i}", [128, 512], F32)) for i in range(2)]
    ab_ = [E(nc.sbuf_tensor(f"abf{i}", [128, 512], F32)) for i in range(2)]
    fq = [E(nc.sbuf_tensor(f"fq{i}", [128, 512], BF16)) for i in range(2)]
    aq = [E(nc.sbuf_tensor(f"aq{i}", [128, 512], BF16)) for i in range(2)]
    stat = E(nc.sbuf_tensor("stat", [128, 8, G], F32))
    SSUM, SACC, SM, SMSQ, SVAR, SLNV, SRSTD, SNMR = range(8)
    kscol = E(nc.sbuf_tensor("kscol", [128, 8], F32))
    rrec = E(nc.sbuf_tensor("rrec", [8, 128], F32))

    pq = [E(nc.psum_tensor(f"pq{i}", [128, 512], F32)) for i in range(2)]
    pa = [E(nc.psum_tensor(f"pa{i}", [128, 512], F32)) for i in range(2)]

    S = {}
    N = {}
    COMPUTE = ("PE", "ACT", "DVE", "PL", "CC", "B")
    DMA_SEMS = (
        "XL", "TL", "WDL", "STRA", "STRB", "BC",
        "KT0", "KT1", "GPP0", "GPP1", "XPP0", "XPP1", "ARL",
        "XNS", "WDS", "KST0", "KST1", "GST0", "GST1", "AST0", "AST1", "CCS",
        "CWK", "CWV",
        "CQ0", "CQ1", "CQ2", "CQ3", "CQ4", "CQ5", "CQ6", "CQ7",
        "CO0", "CO1", "CO2", "CO3", "CO4", "CO5", "CO6", "CO7",
        "OUT0", "OUT1",
    )
    for name in COMPUTE + DMA_SEMS:
        S[name] = E(nc.semaphore(name))
        N[name] = 0

    block = E(nc.Block())

    def carve(off_bytes, shape, dt):
        nbytes = int(np.prod(shape[1:])) * mybir.dt.size(dt)
        assert off_bytes % 64 == 0
        assert off_bytes + nbytes <= 24576, (off_bytes, nbytes)
        v = arena.ap()[:, off_bytes:off_bytes + nbytes].bitcast(dt)
        if len(shape) == 2:
            return v
        pat = "p (" + " ".join(f"d{i}" for i in range(1, len(shape))) + ") -> p " + \
              " ".join(f"d{i}" for i in range(1, len(shape)))
        kw = {f"d{i}": shape[i] for i in range(1, len(shape))}
        return v.rearrange(pat, **kw)

    ENGS = ("sync", "gpsimd", "tensor", "scalar", "vector")
    TRACE = {k: [] for k in ENGS}
    nc._plan_trace = TRACE

    class Phase:
        def __init__(self):
            self.ops = {k: [] for k in ENGS}

        def add(self, eng, waits, build, incs=()):
            for sname, k in incs:
                N[sname] += k
            after = {sname: N[sname] for sname, k in incs}
            wl = [(S[sname], v) for sname, v in waits if v > 0]
            il = [(S[sname], k) for sname, k in incs]
            TRACE[eng].append(([(sn, v) for sn, v in waits if v > 0],
                               [(sn, k) for sn, k in incs]))

            def f(e, wl=wl, build=build, il=il):
                for sem, v in wl:
                    e.wait_ge(sem, v)
                ins = build(e)
                for sem, k in il:
                    ins = ins.then_inc(sem, k)
            self.ops[eng].append(f)
            return after

        def dma(self, eng, stream, waits, build):
            self.add(eng, waits, build, [(stream, 16)])
            return (stream, N[stream])

        def flush(self):
            for eng in ENGS:
                lst = self.ops[eng]
                if not lst:
                    continue
                getattr(block, eng)(lambda e, lst=lst: [f(e) for f in lst])
            self.ops = {k: [] for k in ENGS}

    SYNC_STREAMS = ("XL", "TL", "WDL", "STRA", "STRB", "BC", "KT0", "KT1",
                    "GPP0", "GPP1", "XPP0", "XPP1", "ARL")
    GP_STREAMS = tuple(s for s in DMA_SEMS if s not in SYNC_STREAMS)

    def barrier(ph):
        ph.add("sync", [(s, N[s]) for s in SYNC_STREAMS], lambda e: e.nop(), [("B", 1)])
        ph.add("gpsimd", [(s, N[s]) for s in GP_STREAMS], lambda e: e.nop(), [("B", 1)])
        for eng in ("tensor", "scalar", "vector"):
            ph.add(eng, [], lambda e: e.nop(), [("B", 1)])
        tgt = N["B"]
        for eng in ENGS:
            ph.add(eng, [("B", tgt)], lambda e: e.nop())

    def load_full_set(e):
        return e.add_instruction(mybir.InstLoadActFuncSet(
            name=nc.get_next_instruction_name(), act_func_set_id=6, ins=[], outs=[]))

    R = {}

    ph = Phase()
    ph.dma("sync", "BC", [], lambda e: e.dma_start(out=ident_sb[:, :], in_=t_id.ap()))
    tok_bq = ph.dma("sync", "BC", [], lambda e: e.dma_start(
        out=bq_nat[:, :], in_=t_bq.ap().rearrange("(c p) -> c p", p=128)))
    R["cast_wk"] = ph.dma("gpsimd", "CWK", [], lambda e: e.dma_start(out=wk_bf.ap(), in_=t_wk.ap()))
    R["cast_wv"] = ph.dma("gpsimd", "CWV", [], lambda e: e.dma_start(out=wv_bf.ap(), in_=t_wv.ap()))
    ph.add("vector", [], lambda e: e.memset(ones_sb[:, :], 1.0), [("DVE", 1)])
    pe1 = ph.add("tensor", [tok_bq],
                 lambda e: e.matmul(pa[0][:, 0:32], bq_nat[:, :], ident_sb[0:32, 0:32],
                                    start=True, stop=True), [("PE", 1)])["PE"]
    ph.add("vector", [("PE", pe1)],
           lambda e: e.tensor_copy(bq_cols[:, :], pa[0][:, 0:32]), [("DVE", 1)])
    ph.add("scalar", [], lambda e: load_full_set(e))
    barrier(ph)
    ph.flush()

    if upto < 1:
        ES.close()
        return nc

    ph = Phase()
    ph.add("scalar", [], lambda e: load_full_set(e))
    xbuf = carve(0, [128, D], F32)
    xn_tm = carve(16384, [128, D], BF16)
    inv_d = 1.0 / D
    ident_c = {}
    store_tok = {}
    for g in range(G):
        xl = ph.dma("sync", "XL", [("ACT", ident_c.get(g - 1, 0))],
                    lambda e, g=g: e.dma_start(out=xbuf, in_=t_x.ap()[g * 128:(g + 1) * 128, :]))
        red = ph.add("vector", [xl], lambda e, g=g: e.reduce_sum(
            stat[:, SSUM, g:g + 1], xbuf, axis=AX.X), [("DVE", 1)])["DVE"]
        w = [xl] + ([store_tok[g - 1]] if g >= 1 else [])
        sq = ph.add("scalar", w, lambda e, g=g: e.activation(
            xn_tm, xbuf, AF.Square, accum_out=stat[:, SACC, g:g + 1]), [("ACT", 1)])["ACT"]
        m = ph.add("vector", [("DVE", red)], lambda e, g=g: e.tensor_scalar_mul(
            stat[:, SM, g:g + 1], stat[:, SSUM, g:g + 1], inv_d), [("DVE", 1)])["DVE"]
        msq = ph.add("vector", [("DVE", m), ("ACT", sq)], lambda e, g=g: e.tensor_scalar(
            stat[:, SMSQ, g:g + 1], stat[:, SM, g:g + 1], stat[:, SM, g:g + 1],
            float(-EPS_LN), ALU.mult, ALU.add), [("DVE", 1)])["DVE"]
        var = ph.add("vector", [("DVE", msq)], lambda e, g=g: e.scalar_tensor_tensor(
            stat[:, SVAR, g:g + 1], stat[:, SACC, g:g + 1], inv_d, stat[:, SMSQ, g:g + 1],
            ALU.mult, ALU.subtract), [("DVE", 1)])["DVE"]
        ln = ph.add("scalar", [("DVE", var)], lambda e, g=g: e.activation(
            stat[:, SLNV, g:g + 1], stat[:, SVAR, g:g + 1], AF.Ln), [("ACT", 1)])["ACT"]
        nh = ph.add("vector", [("ACT", ln)], lambda e, g=g: e.tensor_scalar_mul(
            stat[:, SLNV, g:g + 1], stat[:, SLNV, g:g + 1], -0.5), [("DVE", 1)])["DVE"]
        rstd = ph.add("scalar", [("DVE", nh)], lambda e, g=g: e.activation(
            stat[:, SRSTD, g:g + 1], stat[:, SLNV, g:g + 1], AF.Exp), [("ACT", 1)])["ACT"]
        nmr = ph.add("vector", [("ACT", rstd)], lambda e, g=g: e.scalar_tensor_tensor(
            stat[:, SNMR, g:g + 1], stat[:, SM, g:g + 1], -1.0, stat[:, SRSTD, g:g + 1],
            ALU.mult, ALU.mult), [("DVE", 1)])["DVE"]
        ident_c[g] = ph.add("scalar", [("DVE", nmr)], lambda e, g=g: e.activation(
            xn_tm, xbuf, AF.Identity,
            bias=stat[:, SNMR, g:g + 1], scale=stat[:, SRSTD, g:g + 1]), [("ACT", 1)])["ACT"]
        store_tok[g] = ph.dma("gpsimd", "XNS", [("ACT", ident_c[g])],
                              lambda e, g=g: e.dma_start(
                                  out=xn_dram.ap()[g * 128:(g + 1) * 128, :], in_=xn_tm))
        if g % 4 == 3:
            tb = g // 4
            for ic in range(IC):
                tok = ph.dma("sync", "TL", [store_tok[g]] if ic == 0 else [],
                             lambda e, tb=tb, ic=ic: e.dma_start(
                                 out=slab[:, ic, tb * 512:(tb + 1) * 512],
                                 in_=xn_dram.ap()[tb * 512:(tb + 1) * 512, ic * 128:(ic + 1) * 128],
                                 transpose=True))
            R["slab_ready"] = tok
    barrier(ph)
    ph.flush()

    if upto < 2:
        ES.close()
        return nc

    ph = Phase()
    wd_int = carve(0, [128, 32, 32], I32)
    wd_out = carve(4096, [128, 32, 32], BF16)
    wd_sc = carve(6208, [128, 32], F32)
    dq_c = {}
    wds_tok = {}
    for i in range(128):
        os_, cc = divmod(i, 4)
        ph.dma("sync", "WDL", [("DVE", dq_c.get(i - 1, 0))],
               lambda e, os_=os_, cc=cc: e.dma_start(
                   out=wd_int, in_=t_wdq.ap()[os_ * 128:(os_ + 1) * 128, cc * 1024:(cc + 1) * 1024]
                   .rearrange("p (b k) -> p b k", k=32)))
        dl = ph.dma("sync", "WDL", [], lambda e, os_=os_, cc=cc: e.dma_start(
            out=wd_sc, in_=t_sc.ap()[os_ * 128:(os_ + 1) * 128, cc * 32:(cc + 1) * 32]))
        dq_c[i] = ph.add("vector", [dl] + ([wds_tok[i - 1]] if i >= 1 else []),
                         lambda e: e.scalar_tensor_tensor(
                             wd_out, wd_int, -128.0,
                             wd_sc.unsqueeze(2).broadcast_to([128, 32, 32]),
                             ALU.add, ALU.mult), [("DVE", 1)])["DVE"]
        wds_tok[i] = ph.dma("gpsimd", "WDS", [("DVE", dq_c[i])],
                            lambda e, os_=os_, cc=cc: e.dma_start(
                                out=wd_bf.ap()[os_ * 128:(os_ + 1) * 128, cc * 1024:(cc + 1) * 1024],
                                in_=wd_out.rearrange("p b k -> p (b k)")))
        if i % 16 == 0:
            blk = i // 16
            R[f"cast_wq{blk}"] = ph.dma("gpsimd", f"CQ{blk}", [],
                                        lambda e, blk=blk: e.dma_start(
                                            out=wq_bf.ap()[blk * 512:(blk + 1) * 512, :],
                                            in_=t_wq.ap()[blk * 512:(blk + 1) * 512, :]))
    barrier(ph)
    ph.flush()

    if upto < 3:
        ES.close()
        return nc

    ph = Phase()
    ph.add("scalar", [], lambda e: load_full_set(e))
    chunks = [(True, 0), (True, 1), (False, 0), (False, 1)]
    pq_free = {}
    phi_c = {}
    ract_c = {}
    kst_tok = {}
    last_mm = 0
    for ci, (is_k, half) in enumerate(chunks):
        wsrc = wk_bf if is_k else wv_bf
        bsrc = t_bk if is_k else t_bv
        bl = ph.dma("sync", "BC", [("PE", last_mm)],
                    lambda e, bsrc=bsrc, half=half: e.dma_start(
                        out=b_cur[:, :], in_=bsrc.ap()[half * 512:(half + 1) * 512].unsqueeze(0)))
        stlA = stlB = None
        for ic in range(IC):
            w = []
            if ic == 0:
                w = [("PE", last_mm), R["cast_wk"] if is_k else R["cast_wv"]]
            tok = ph.dma("sync", "STRA" if ic < 16 else "STRB", w,
                         lambda e, wsrc=wsrc, half=half, ic=ic: e.dma_start(
                             out=wstrip[:, ic, :],
                             in_=wsrc.ap()[half * 512:(half + 1) * 512, ic * 128:(ic + 1) * 128],
                             transpose=True))
            if ic == 15:
                stlA = tok
            elif ic == 31:
                stlB = tok
        for g in range(G):
            ti = ci * G + g
            p = ti % 2
            q = ti % 2
            w = [stlA, bl]
            if (ti - 2) in pq_free:
                w.append(pq_free[ti - 2])
            if ti == 0:
                w.append(R["slab_ready"])
            for ic in range(IC):
                ww = w if ic == 0 else ([stlB] if (ic == 16 and g == 0) else [])
                ph.add("tensor", ww, lambda e, p=p, ic=ic, g=g: e.matmul(
                    pq[p][:, :], slab[:, ic, g * 128:(g + 1) * 128], wstrip[:, ic, :],
                    start=(ic == 0), stop=False))
            mm = ph.add("tensor", [], lambda e, p=p: e.matmul(
                pq[p][:, :], ones_sb[:, :], b_cur[:, :], start=False, stop=True),
                [("PE", 1)])["PE"]
            last_mm = mm
            if is_k:
                ucp = ph.add("vector", [("PE", mm), ("ACT", ract_c.get(ti - 2, 0))],
                             lambda e, p=p, q=q: e.tensor_copy(ub[q][:, :], pq[p][:, :]),
                             [("DVE", 1)])["DVE"]
                pq_free[ti] = ("DVE", ucp)
                ph.add("scalar", [("DVE", ucp), ("DVE", phi_c.get(ti - 2, 0))],
                       lambda e, q=q: e.activation(eb[q][:, :], ub[q][:, :], AF.Exp))
                ract_c[ti] = ph.add("scalar", [], lambda e, q=q: e.activation(
                    ab_[q][:, :], ub[q][:, :], AF.Relu), [("ACT", 1)])["ACT"]
                wph = [("ACT", ract_c[ti])]
                if (ti - 2) in kst_tok:
                    wph.append(kst_tok[ti - 2])
                phi_c[ti] = ph.add("vector", wph,
                                   lambda e, q=q: e.scalar_tensor_tensor(
                                       fq[q][:, :], eb[q][:, :], 1.0, ab_[q][:, :],
                                       ALU.min, ALU.add), [("DVE", 1)])["DVE"]
                done = phi_c[ti]
            else:
                wv_ = [("PE", mm)]
                if (ti - 2) in kst_tok:
                    wv_.append(kst_tok[ti - 2])
                done = ph.add("vector", wv_,
                              lambda e, p=p, q=q: e.tensor_copy(
                                  fq[q][:, :], pq[p][:, :]), [("DVE", 1)])["DVE"]
                pq_free[ti] = ("DVE", done)
            dst = kphi_dram if is_k else vbf_dram
            kst_tok[ti] = ph.dma("gpsimd", f"KST{ti % 2}", [("DVE", done)],
                                 lambda e, g=g, half=half, q=q, dst=dst: e.dma_start(
                                     out=dst.ap()[g * 128:(g + 1) * 128, half * 512:(half + 1) * 512],
                                     in_=fq[q][:, :]))
    barrier(ph)
    ph.flush()

    if upto < 4:
        ES.close()
        return nc

    ph = Phase()
    kt = [carve(0, [128, KF], BF16), carve(2048, [128, KF], BF16)]
    vaug = [carve(4096, [128, 8, 144], BF16), carve(6400, [128, 8, 144], BF16)]
    kv_acc = carve(8704, [128, 1032], F32)
    with nc.psum_tensor("pkv", [128, 1024], F32) as pkv, \
         nc.psum_tensor("pks", [128, 8], F32) as pks:
        ph.add("vector", [], lambda e: e.memset(kv_acc, 0.0), [("DVE", 1)])
        ph.add("vector", [], lambda e: e.memset(vaug[0][:, :, 128:144], 1.0), [("DVE", 1)])
        mset = ph.add("vector", [], lambda e: e.memset(vaug[1][:, :, 128:144], 1.0), [("DVE", 1)])["DVE"]
        kvg = {}
        accg = {}
        for g in range(G):
            b = g % 2
            ph.dma("sync", f"KT{b}", [("PE", kvg.get(g - 2, 0))],
                   lambda e, g=g, b=b: e.dma_start(
                       out=kt[b], in_=kphi_dram.ap()[g * 128:(g + 1) * 128, :]))
            ktl = ph.dma("sync", f"KT{b}", [], lambda e, g=g, b=b: e.dma_start(
                out=vaug[b][:, :, 0:128],
                in_=vbf_dram.ap()[g * 128:(g + 1) * 128, :].rearrange("l (h v) -> l h v", h=8)))
            w = [ktl, ("DVE", accg.get(g - 1, mset))]
            for h in range(NKV):
                ph.add("tensor", w if h == 0 else [],
                       lambda e, b=b, h=h: e.matmul(
                           pkv.ap()[:, h * 128:(h + 1) * 128],
                           kt[b][:, h * 128:(h + 1) * 128],
                           vaug[b][:, h, 0:128], start=True, stop=True))
                kvg[g] = ph.add("tensor", [],
                                lambda e, b=b, h=h: e.matmul(
                                    pks.ap()[:, h:h + 1],
                                    kt[b][:, h * 128:(h + 1) * 128],
                                    vaug[b][:, h, 128:129], start=True, stop=True),
                                [("PE", 1)])["PE"]
            ph.add("vector", [("PE", kvg[g])], lambda e: e.tensor_tensor(
                kv_acc[:, 0:1024], kv_acc[:, 0:1024], pkv.ap()[:, :], ALU.add), [("DVE", 1)])
            accg[g] = ph.add("vector", [], lambda e: e.tensor_tensor(
                kv_acc[:, 1024:1032], kv_acc[:, 1024:1032], pks.ap()[:, :], ALU.add),
                [("DVE", 1)])["DVE"]
        ccst = ph.dma("gpsimd", "CCS", [("DVE", accg[G - 1])],
                      lambda e: e.dma_start(out=cc_in.ap(), in_=kv_acc))
        ph.add("gpsimd", [ccst], lambda e: e.collective_compute(
            "AllReduce", ALU.add, replica_groups=[list(range(NCORE))],
            ins=[cc_in.ap()], outs=[cc_out.ap()]), [("CC", 1)])
        ph.flush()

    if upto < 5:
        ES.close()
        return nc

    ph = Phase()
    ph.add("scalar", [], lambda e: e.preload_act_table(AF.Silu))
    silu_c = {}
    gst_tok = {}
    last_mm_f = 0
    idx = 0
    for ow in range(8):
        stlA = stlB = None
        for ic in range(IC):
            w = [("PE", last_mm_f)] if ic == 0 else []
            tok = ph.dma("sync", "STRA" if ic < 16 else "STRB", w,
                         lambda e, ow=ow, ic=ic: e.dma_start(
                             out=wstrip[:, ic, :],
                             in_=wd_bf.ap()[ow * 512:(ow + 1) * 512, ic * 128:(ic + 1) * 128],
                             transpose=True))
            if ic == 15:
                stlA = tok
            elif ic == 31:
                stlB = tok
        for g in range(G):
            p = idx % 2
            w = [stlA, ("ACT", silu_c.get(idx - 2, 0))]
            for ic in range(IC):
                inc = [("PE", 1)] if ic == IC - 1 else ()
                ww = w if ic == 0 else ([stlB] if (ic == 16 and g == 0) else [])
                ph.add("tensor", ww, lambda e, p=p, ic=ic, g=g: e.matmul(
                    pq[p][:, :], slab[:, ic, g * 128:(g + 1) * 128], wstrip[:, ic, :],
                    start=(ic == 0), stop=(ic == IC - 1)), inc)
            last_mm_f = N["PE"]
            wsl = [("PE", last_mm_f)]
            if (idx - 2) in gst_tok:
                wsl.append(gst_tok[idx - 2])
            silu_c[idx] = ph.add("scalar", wsl, lambda e, p=p: e.activation(
                eb[p][:, :], pq[p][:, :], AF.Silu), [("ACT", 1)])["ACT"]
            gst_tok[idx] = ph.dma("gpsimd", f"GST{idx % 2}", [("ACT", silu_c[idx])],
                                  lambda e, ow=ow, g=g, p=p: e.dma_start(
                                      out=g_dram.ap()[g * 128:(g + 1) * 128, ow * 512:(ow + 1) * 512],
                                      in_=eb[p][:, :]))
            idx += 1
        R[f"cast_wo{ow}"] = ph.dma("gpsimd", f"CO{ow}", [], lambda e, ow=ow: e.dma_start(
            out=wo_bf.ap()[ow * 512:(ow + 1) * 512, :],
            in_=t_wo.ap()[ow * 512:(ow + 1) * 512, :]))
    R["ffn_last_silu"] = silu_c[idx - 1]
    barrier(ph)
    ph.flush()

    if upto < 6:
        ES.close()
        return nc

    ph = Phase()
    kv_all = carve(13312, [128, 1032], F32)
    kvl = ph.dma("sync", "BC", [("CC", 1)], lambda e: e.dma_start(
        out=kv_all, in_=cc_out.ap()))
    c1 = ph.add("vector", [kvl], lambda e: e.tensor_copy(
        kscol[:, :], kv_all[:, 1024:1032]), [("DVE", 1)])["DVE"]
    p1 = ph.add("tensor", [("DVE", c1)], lambda e: e.matmul(
        pa[0][0:8, 0:128], kscol[:, :], ident_sb[:, :], start=True, stop=True), [("PE", 1)])["PE"]
    c2 = ph.add("vector", [("PE", p1)], lambda e: e.tensor_scalar_add(
        rrec[:, :], pa[0][0:8, 0:128], float(EPS_NRM)), [("DVE", 1)])["DVE"]
    c3 = ph.add("vector", [("DVE", c2)], lambda e: e.reciprocal(rrec[:, :], rrec[:, :]),
                [("DVE", 1)])["DVE"]
    g1 = ph.dma("gpsimd", "CCS", [("DVE", c3)],
                lambda e: e.dma_start(out=rbounce.ap(), in_=rrec[:, :]))
    rflat = rbounce.ap().rearrange("h v -> (h v)")
    l2 = ph.dma("sync", "BC", [g1], lambda e: e.dma_start(
        out=b_cur[:, :], in_=rflat[0:512].unsqueeze(0)))
    p2 = ph.add("tensor", [l2, ("ACT", R["ffn_last_silu"])], lambda e: e.matmul(
        pq[0][:, :], ones_sb[:, :], b_cur[:, :], start=True, stop=True), [("PE", 1)])["PE"]
    l3 = ph.dma("sync", "BC", [("PE", p2)], lambda e: e.dma_start(
        out=b_cur[:, :], in_=rflat[512:1024].unsqueeze(0)))
    p3 = ph.add("tensor", [l3], lambda e: e.matmul(
        pq[1][:, :], ones_sb[:, :], b_cur[:, :], start=True, stop=True), [("PE", 1)])["PE"]
    ph.add("vector", [("PE", p3)], lambda e: e.tensor_tensor(
        kvp[:, 0:4, :], kv_all[:, 0:512].rearrange("p (h v) -> p h v", h=4),
        pq[0].ap().rearrange("p (h v) -> p h v", h=4), ALU.mult), [("DVE", 1)])
    kvp_done = ph.add("vector", [], lambda e: e.tensor_tensor(
        kvp[:, 4:8, :], kv_all[:, 512:1024].rearrange("p (h v) -> p h v", h=4),
        pq[1].ap().rearrange("p (h v) -> p h v", h=4), ALU.mult), [("DVE", 1)])["DVE"]
    R["kvp_done"] = kvp_done
    barrier(ph)
    ph.flush()

    if upto < 7:
        ES.close()
        return nc

    ph = Phase()
    ph.add("scalar", [], lambda e: load_full_set(e))
    qrelu = {}
    pqa_free = {}
    phiq = {}
    amm = {}
    abcp = {}
    ast_tok = {}
    last_qmm = 0
    pending = None
    idx = 0

    def emit_attn(pidx, h_, pp, qq):
        wa = [("DVE", phiq[pidx]), ("DVE", abcp.get(pidx - 2, 0))]
        if pidx < 2:
            wa.append(("DVE", R["kvp_done"]))
        amm[pidx] = ph.add("tensor", wa,
                           lambda e, h_=h_, pp=pp, qq=qq: e.matmul(
                               pa[pp][:, :], kvp[:, h_ // 4, :], fq[qq][:, :],
                               start=True, stop=True), [("PE", 1)])["PE"]
        wc = [("PE", amm[pidx])]
        if (pidx - 2) in ast_tok:
            wc.append(ast_tok[pidx - 2])
        abcp[pidx] = ph.add("vector", wc,
                            lambda e, pp=pp, qq=qq: e.tensor_copy(aq[qq][:, :], pa[pp][:, :]),
                            [("DVE", 1)])["DVE"]
        pht, ptc = divmod(pidx, 4)
        ast_tok[pidx] = ph.dma("gpsimd", f"AST{pidx % 2}", [("DVE", abcp[pidx])],
                               lambda e, pht=pht, ptc=ptc, qq=qq: e.dma_start(
                                   out=attn_dram.ap()[pht, :, ptc * 512:(ptc + 1) * 512],
                                   in_=aq[qq][:, :]))

    for hg in range(8):
        stlA = stlB = None
        for ic in range(IC):
            w = []
            if ic == 0:
                w = [("PE", last_qmm), R[f"cast_wq{hg}"]]
            tok = ph.dma("sync", "STRA" if ic < 16 else "STRB", w,
                         lambda e, hg=hg, ic=ic: e.dma_start(
                             out=wstrip[:, ic, :],
                             in_=wq_bf.ap()[hg * 512:(hg + 1) * 512, ic * 128:(ic + 1) * 128],
                             transpose=True))
            if ic == 15:
                stlA = tok
            elif ic == 31:
                stlB = tok
        for h4 in range(4):
            h = hg * 4 + h4
            for tc in range(4):
                p = idx % 2
                q = idx % 2
                w = [stlA]
                if (idx - 2) in pqa_free:
                    w.append(pqa_free[idx - 2])
                if idx < 2:
                    w.append(("DVE", R["kvp_done"]))
                first_tile = (h4 == 0 and tc == 0)
                for ic in range(IC):
                    inc = [("PE", 1)] if ic == IC - 1 else ()
                    ww = w if ic == 0 else ([stlB] if (ic == 16 and first_tile) else [])
                    ph.add("tensor", ww, lambda e, p=p, ic=ic, h4=h4, tc=tc: e.matmul(
                        pq[p][:, :], wstrip[:, ic, h4 * 128:(h4 + 1) * 128],
                        slab[:, ic, tc * 512:(tc + 1) * 512],
                        start=(ic == 0), stop=(ic == IC - 1)), inc)
                qmm = N["PE"]
                last_qmm = qmm
                if pending is not None:
                    emit_attn(*pending)
                qcp = ph.add("vector", [("PE", qmm), ("ACT", qrelu.get(idx - 2, 0))],
                             lambda e, p=p, q=q, h=h: e.tensor_scalar_add(
                                 ub[q][:, :], pq[p][:, :], bq_cols[:, h:h + 1]),
                             [("DVE", 1)])["DVE"]
                pqa_free[idx] = ("DVE", qcp)
                ph.add("scalar", [("DVE", qcp), ("DVE", phiq.get(idx - 2, 0))],
                       lambda e, q=q: e.activation(eb[q][:, :], ub[q][:, :], AF.Exp))
                qrelu[idx] = ph.add("scalar", [], lambda e, q=q: e.activation(
                    ab_[q][:, :], ub[q][:, :], AF.Relu), [("ACT", 1)])["ACT"]
                phiq[idx] = ph.add("vector", [("ACT", qrelu[idx]), ("PE", amm.get(idx - 2, 0))],
                                   lambda e, q=q: e.scalar_tensor_tensor(
                                       fq[q][:, :], eb[q][:, :], 1.0, ab_[q][:, :],
                                       ALU.min, ALU.add), [("DVE", 1)])["DVE"]
                pending = (idx, h, p, q)
                idx += 1
    emit_attn(*pending)
    barrier(ph)
    ph.flush()

    if upto < 8:
        ES.close()
        return nc

    ph = Phase()
    gp_piece = [carve(0, [128, 512], F32), carve(2048, [128, 512], F32)]
    xp_piece = [carve(4096, [128, 512], F32), carve(6144, [128, 512], F32)]
    outb = [carve(8192, [128, 512], F32), carve(10240, [128, 512], F32)]
    arl = None
    for tc in range(4):
        arl = ph.dma("sync", "ARL", [], lambda e, tc=tc: e.dma_start(
            out=slab[:, :, tc * 512:(tc + 1) * 512],
            in_=attn_dram.ap()[:, :, tc * 512:(tc + 1) * 512].rearrange("fc p t -> p fc t")))
    prod = {}
    outtt = {}
    out_tok = {}
    last_smm = 0
    idx = 0
    for ow in range(8):
        bl = ph.dma("sync", "BC", [("PE", last_smm)], lambda e, ow=ow: e.dma_start(
            out=b_cur[:, :], in_=t_bo.ap()[ow * 512:(ow + 1) * 512].unsqueeze(0)))
        stlA = stlB = None
        for fc in range(IC):
            w = []
            if fc == 0:
                w = [("PE", last_smm), R[f"cast_wo{ow}"]]
            tok = ph.dma("sync", "STRA" if fc < 16 else "STRB", w,
                         lambda e, ow=ow, fc=fc: e.dma_start(
                             out=wstrip[:, fc, :],
                             in_=wo_bf.ap()[ow * 512:(ow + 1) * 512, fc * 128:(fc + 1) * 128],
                             transpose=True))
            if fc == 15:
                stlA = tok
            elif fc == 31:
                stlB = tok
        for g in range(G):
            p = idx % 2
            q = idx % 2
            gl = ph.dma("sync", f"GPP{q}", [("DVE", prod.get(idx - 2, 0))],
                        lambda e, ow=ow, g=g, q=q: e.dma_start(
                            out=gp_piece[q],
                            in_=g_dram.ap()[g * 128:(g + 1) * 128, ow * 512:(ow + 1) * 512]))
            xl = ph.dma("sync", f"XPP{q}", [("PL", outtt.get(idx - 2, 0))],
                        lambda e, ow=ow, g=g, q=q: e.dma_start(
                            out=xp_piece[q],
                            in_=t_x.ap()[g * 128:(g + 1) * 128, ow * 512:(ow + 1) * 512]))
            w = [stlA, bl, ("DVE", prod.get(idx - 2, 0))]
            if idx == 0:
                w.append(arl)
            for fc in range(IC):
                ww = w if fc == 0 else ([stlB] if (fc == 16 and g == 0) else [])
                ph.add("tensor", ww, lambda e, p=p, fc=fc, g=g: e.matmul(
                    pq[p][:, :], slab[:, fc, g * 128:(g + 1) * 128], wstrip[:, fc, :],
                    start=(fc == 0), stop=False))
            smm = ph.add("tensor", [], lambda e, p=p: e.matmul(
                pq[p][:, :], ones_sb[:, :], b_cur[:, :], start=False, stop=True),
                [("PE", 1)])["PE"]
            last_smm = smm
            prod[idx] = ph.add("vector", [("PE", smm), gl, ("PL", outtt.get(idx - 2, 0))],
                               lambda e, p=p, q=q: e.tensor_tensor(
                                   ub[q][:, :], pq[p][:, :], gp_piece[q], ALU.mult),
                               [("DVE", 1)])["DVE"]
            wo_ = [("DVE", prod[idx]), xl]
            if (idx - 2) in out_tok:
                wo_.append(out_tok[idx - 2])
            outtt[idx] = ph.add("gpsimd", wo_,
                                lambda e, q=q: e.tensor_tensor(
                                    outb[q], ub[q][:, :], xp_piece[q], ALU.add), [("PL", 1)])["PL"]
            out_tok[idx] = ph.dma("gpsimd", f"OUT{idx % 2}", [("PL", outtt[idx])],
                                  lambda e, ow=ow, g=g, q=q: e.dma_start(
                                      out=t_out.ap()[g * 128:(g + 1) * 128, ow * 512:(ow + 1) * 512],
                                      in_=outb[q]))
            idx += 1
    ph.add("gpsimd", [("OUT0", N["OUT0"]), ("OUT1", N["OUT1"])], lambda e: e.nop())
    ph.flush()

    ES.close()
    return nc


_NC_CACHE = None


def kernel(L0_attn_hidden_states, L0_attn_Wq, L0_attn_bq, L0_attn_Wk, L0_attn_bk,
           L0_attn_Wv, L0_attn_bv, L0_attn_Wo, L0_attn_bo, L0_ffn_w_q, L0_ffn_scales):
    global _NC_CACHE
    x = np.ascontiguousarray(np.asarray(L0_attn_hidden_states, dtype=np.float32).reshape(16384, 4096))
    wq = np.ascontiguousarray(np.asarray(L0_attn_Wq, dtype=np.float32))
    bq = np.ascontiguousarray(np.asarray(L0_attn_bq, dtype=np.float32))
    wk = np.ascontiguousarray(np.asarray(L0_attn_Wk, dtype=np.float32))
    bk = np.ascontiguousarray(np.asarray(L0_attn_bk, dtype=np.float32))
    wv = np.ascontiguousarray(np.asarray(L0_attn_Wv, dtype=np.float32))
    bv = np.ascontiguousarray(np.asarray(L0_attn_bv, dtype=np.float32))
    wo = np.ascontiguousarray(np.asarray(L0_attn_Wo, dtype=np.float32))
    bo = np.ascontiguousarray(np.asarray(L0_attn_bo, dtype=np.float32))
    wdq = np.ascontiguousarray(np.asarray(L0_ffn_w_q, dtype=np.int32))
    sc = np.ascontiguousarray(np.asarray(L0_ffn_scales, dtype=np.float32))
    ident = np.eye(128, dtype=np.float32)

    if _NC_CACHE is None:
        _NC_CACHE = build_nc()
    nc = _NC_CACHE

    in_maps = []
    for c in range(NCORE):
        in_maps.append({
            "x": x[c * T:(c + 1) * T],
            "wq": wq, "bq": bq, "wk": wk, "bk": bk, "wv": wv, "bv": bv,
            "wo": wo, "bo": bo, "wdq": wdq, "scales": sc, "ident": ident,
        })
    res = run_bass_kernel_spmd(nc, in_maps, core_ids=list(range(NCORE)))
    out = np.concatenate([np.asarray(res.results[c]["out"]) for c in range(NCORE)], axis=0)
    return out.reshape(1, 16384, 4096)
